# version 01
# baseline (speedup 1.0000x reference)
import math
import numpy as np
import ml_dtypes
import concourse.bass as bass
import concourse.mybir as mybir
from concourse.bass_utils import run_bass_kernel_spmd
from concourse.bass_core_rust import IndirectOffsetOnAxis

F32 = mybir.dt.float32
BF16 = mybir.dt.bfloat16
I32 = mybir.dt.int32
AF = mybir.ActivationFunctionType
ALU = mybir.AluOpType
AX = mybir.AxisListType
BF = ml_dtypes.bfloat16

B = 8
T = 8192
D = 1024
KT = 8
HID = 4096
MT = 32
NE = 9
H = 16
HD = 64
CH = 512
NCH = 16
TPC = 4
EPS = 1e-5
QSCALE = float(HD) ** -0.5
NTAU = NCH * TPC
POOL_PC = 104


def hilbert_perm(M: int) -> np.ndarray:
    g = int(math.ceil(math.sqrt(M)))
    grid = np.arange(g * g).reshape(g, g)
    grid[1::2] = grid[1::2, ::-1]
    flat = grid.reshape(-1)
    flat = flat[flat < M][:M]
    return flat.astype(np.int32)


def fj_xgt(tt, k): return 8 * tt + k
def fj_qt(ft): return 32 + ft
def fj_kt(ft): return 40 + ft
def fj_vt(seg, nh): return 48 + 2 * seg + nh
def fj_wo(tt, dh): return 56 + 12 * tt + dh
def fj_y1t(tt, k): return 56 + 12 * tt + 2 + k
def fj_rt(tt): return 56 + 12 * tt + 10
def fj_gt(tt): return 56 + 12 * tt + 11


class CSem:
    def __init__(self, nc, name):
        self.h = nc.alloc_semaphore(name)
        self.n = 0

    def inc(self, inst, k=1):
        inst.then_inc(self.h, k)
        self.n += k
        return self.n


def build_program(test_ffn_only=False):
    nc = bass.Bass("TRN2", target_bir_lowering=False)

    x = nc.dram_tensor("x", [T, D], F32, kind="ExternalInput").ap()
    idxT = nc.dram_tensor("idxT", [128, NTAU], I32, kind="ExternalInput").ap()
    ident32 = nc.dram_tensor("ident32", [128, 128], F32, kind="ExternalInput").ap()
    identbf = nc.dram_tensor("identbf", [128, 128], BF16, kind="ExternalInput").ap()
    wqkvT32 = nc.dram_tensor("wqkvT32", [128, KT * 3072], F32, kind="ExternalInput").ap()
    woutT32 = nc.dram_tensor("woutT32", [128, KT * 1024], F32, kind="ExternalInput").ap()
    wrT32 = nc.dram_tensor("wrT32", [128, KT * 8], F32, kind="ExternalInput").ap()
    bfcT32 = nc.dram_tensor("bfcT32", [128, NE * MT], F32, kind="ExternalInput").ap()
    cst32 = nc.dram_tensor("cst32", [128, 8], F32, kind="ExternalInput").ap()
    bprojT32 = nc.dram_tensor("bprojT32", [9, D], F32, kind="ExternalInput").ap()
    WHALF = D * HID // 2
    w1t = nc.dram_tensor("w1t", [NE, 2, WHALF], F32, kind="ExternalInput").ap()
    w2t = nc.dram_tensor("w2t", [NE, 2, WHALF], F32, kind="ExternalInput").ap()
    out = nc.dram_tensor("out", [T, D], F32, kind="ExternalOutput").ap()

    w1s = nc.dram_tensor("w1s", [NE, 2, WHALF], BF16, kind="Internal").ap()
    w2s = nc.dram_tensor("w2s", [NE, 2, WHALF], BF16, kind="Internal").ap()

    def w1_slice(e, m):
        h, ml = divmod(m, 16)
        blk = KT * 128 * 128
        return w1s[e, h][ml * blk:(ml + 1) * blk].rearrange("(p c) -> p c", p=128)

    def w2_slice(e, q):
        h, ql = divmod(q, 4)
        blk = 4 * 1024 * 128
        return w2s[e, h][ql * blk:(ql + 1) * blk].rearrange("(p c) -> p c", p=128)
    y1kind = "ExternalInput" if test_ffn_only else "Internal"
    y1Ts = nc.dram_tensor("y1Ts", [NCH, 128, KT * CH], BF16, kind=y1kind).ap()
    y1nat = nc.dram_tensor("y1nat", [T, D], F32, kind=y1kind).ap()
    if test_ffn_only:
        gate_in = nc.dram_tensor("gate_in", [128, NTAU * 8], F32, kind="ExternalInput").ap()
        gateT_in = nc.dram_tensor("gateT_in", [9, T], BF16, kind="ExternalInput").ap()

    SSETS = CSem(nc, "SSETS")
    SSETG = CSem(nc, "SSETG")
    SIDX = CSem(nc, "SIDX")
    SG = [CSem(nc, f"SG{i}") for i in range(3)]
    SXR = [CSem(nc, f"SXR{i}") for i in range(2)]
    SCAST = CSem(nc, "SCAST")
    SY1TW = CSem(nc, "SY1TW")
    SY1NW = [CSem(nc, f"SY1NW{i}") for i in range(2)]
    SW1 = [CSem(nc, f"SW1_{i}") for i in range(4)]
    SW2 = [CSem(nc, f"SW2_{i}") for i in range(3)]
    SY1G = CSem(nc, "SY1G")
    SYN = CSem(nc, "SYN")
    SOUT = [CSem(nc, f"SOUT{i}") for i in range(2)]

    PE_F = CSem(nc, "PE_F")
    PE_SC = CSem(nc, "PE_SC")
    PE_AT = CSem(nc, "PE_AT")
    PE_OT = CSem(nc, "PE_OT")
    V_POOL = CSem(nc, "V_POOL")
    V_NM = CSem(nc, "V_NM")
    S_EXP = CSem(nc, "S_EXP")
    V_ABF = CSem(nc, "V_ABF")
    V_ATS = CSem(nc, "V_ATS")
    V_OTS = CSem(nc, "V_OTS")
    V_D = CSem(nc, "V_D")
    S_SQ = CSem(nc, "S_SQ")
    S_RSTD = CSem(nc, "S_RSTD")
    V_Y1 = CSem(nc, "V_Y1")
    V_NMR = CSem(nc, "V_NMR")
    S_EXPR = CSem(nc, "S_EXPR")
    V_GATE = CSem(nc, "V_GATE")
    VS = CSem(nc, "VS")
    SS = CSem(nc, "SS")

    PE_S1 = CSem(nc, "PE_S1")
    S_GELU = CSem(nc, "S_GELU")
    PE_S2 = CSem(nc, "PE_S2")
    V_FFE = CSem(nc, "V_FFE")
    W2FREE = CSem(nc, "W2FREE")
    V_D2 = CSem(nc, "V_D2")
    S_SQ2 = CSem(nc, "S_SQ2")
    S_RSTD2 = CSem(nc, "S_RSTD2")
    V_OUTR = CSem(nc, "V_OUTR")
    V_FIN = CSem(nc, "V_FIN")

    SETS_TGT = 16 * (6 if test_ffn_only else 4)
    SETG_TGT = 16 * (2 if test_ffn_only else 4)

    def wait_setup(eng):
        eng.wait_ge(SSETS.h, SETS_TGT)
        eng.wait_ge(SSETG.h, SETG_TGT)

    sb_ident32 = nc.alloc_sbuf_tensor("sb_ident32", [128, 128], F32)
    sb_identbf = nc.alloc_sbuf_tensor("sb_identbf", [128, 128], BF16)
    sb_idxT = nc.alloc_sbuf_tensor("sb_idxT", [128, NTAU], I32)
    sb_bfcT = nc.alloc_sbuf_tensor("sb_bfcT", [128, NE * MT], F32)
    sb_cst = nc.alloc_sbuf_tensor("sb_cst", [128, 8], F32)
    sb_bprojT = nc.alloc_sbuf_tensor("sb_bprojT", [9, D], BF16)
    sb_wr = nc.alloc_sbuf_tensor("sb_wr", [128, KT * 8], BF16)
    sb_gaten = nc.alloc_sbuf_tensor("sb_gaten", [128, NTAU * 8], F32)
    sb_gateT = nc.alloc_sbuf_tensor("sb_gateT", [9, T], BF16)

    def _flat2d(ap):
        return ap.rearrange("(r c) -> r c", c=65536)

    cast_jobs = []
    for e in range(NE):
        cast_jobs.append((_flat2d(w1s[e, 0]), _flat2d(w1t[e, 0])))
        cast_jobs.append((_flat2d(w1s[e, 1]), _flat2d(w1t[e, 1])))
        cast_jobs.append((_flat2d(w2s[e, 0]), _flat2d(w2t[e, 0])))
        cast_jobs.append((_flat2d(w2s[e, 1]), _flat2d(w2t[e, 1])))

    def fidx(c, j):
        return POOL_PC * c + j

    import contextlib

    with contextlib.ExitStack() as stA:
        if not test_ffn_only:
            sb_wqkvT = stA.enter_context(nc.sbuf_tensor("sb_wqkvT", [128, KT * 3072], BF16))
            sb_woutT = stA.enter_context(nc.sbuf_tensor("sb_woutT", [128, KT * 1024], BF16))
            sb_xg = [stA.enter_context(nc.sbuf_tensor(f"sb_xg{i}", [128, D], F32)) for i in range(3)]
            sb_xgT = stA.enter_context(nc.sbuf_tensor("sb_xgT", [128, KT * CH], BF16))
            sb_qT = stA.enter_context(nc.sbuf_tensor("sb_qT", [128, KT * CH], BF16))
            sb_kT = stA.enter_context(nc.sbuf_tensor("sb_kT", [128, KT * CH], BF16))
            sb_vnat = stA.enter_context(nc.sbuf_tensor("sb_vnat", [64, 4096], BF16))
            sb_pfp = [stA.enter_context(nc.sbuf_tensor(f"sb_pfp{i}", [128, 64], F32)) for i in range(2)]
            sb_abf = [stA.enter_context(nc.sbuf_tensor(f"sb_abf{i}", [128, 64], BF16)) for i in range(2)]
            sb_atT = [stA.enter_context(nc.sbuf_tensor(f"sb_atT{i}", [64, 128], BF16)) for i in range(2)]
            sb_oT = stA.enter_context(nc.sbuf_tensor("sb_oT", [128, KT * CH], BF16))
            sb_xr = [stA.enter_context(nc.sbuf_tensor(f"sb_xr{i}", [128, D], F32)) for i in range(2)]
            sb_junk = stA.enter_context(nc.sbuf_tensor("sb_junk", [128, D], F32))
            sb_y1 = [stA.enter_context(nc.sbuf_tensor(f"sb_y1{i}", [128, D], F32)) for i in range(2)]
            sb_y1T = stA.enter_context(nc.sbuf_tensor("sb_y1T", [128, KT * CH], BF16))
            sb_ast = [stA.enter_context(nc.sbuf_tensor(f"sb_ast{i}", [128, 4], F32)) for i in range(2)]
            sb_lst = [stA.enter_context(nc.sbuf_tensor(f"sb_lst{i}", [128, 8], F32)) for i in range(2)]
            sb_lg = [stA.enter_context(nc.sbuf_tensor(f"sb_lg{i}", [128, 8], F32)) for i in range(2)]
            sb_rexp = [stA.enter_context(nc.sbuf_tensor(f"sb_rexp{i}", [128, 8], F32)) for i in range(2)]
            sb_rtmp = [stA.enter_context(nc.sbuf_tensor(f"sb_rtmp{i}", [128, 8], F32)) for i in range(2)]
            sb_rst = [stA.enter_context(nc.sbuf_tensor(f"sb_rst{i}", [128, 8], F32)) for i in range(2)]
            sb_gaug = stA.enter_context(nc.sbuf_tensor("sb_gaug", [128, 4 * 9], F32))
            pool_ps = [stA.enter_context(nc.psum_tensor(f"pool_ps{i}", [128, 512], F32)) for i in range(2)]
            sc_ps = [stA.enter_context(nc.psum_tensor(f"sc_ps{i}", [128, 512], F32)) for i in range(2)]
            at_ps = [stA.enter_context(nc.psum_tensor(f"at_ps{i}", [128, 1024], BF16)) for i in range(2)]
            ot_ps = [stA.enter_context(nc.psum_tensor(f"ot_ps{i}", [128, 512], F32)) for i in range(2)]

        with nc.Block() as blk:
            @blk.sync
            def _(sy):
                SSETS.inc(sy.dma_start(sb_ident32[:], ident32), 16)
                SSETS.inc(sy.dma_start(sb_identbf[:], identbf), 16)
                SIDX.inc(sy.dma_start(sb_idxT[:], idxT), 16)
                SSETS.inc(sy.dma_start(sb_bfcT[:], bfcT32), 16)
                SSETS.inc(sy.dma_start(sb_cst[:], cst32), 16)
                if test_ffn_only:
                    SSETS.inc(sy.dma_start(sb_gaten[:], gate_in), 16)
                    SSETS.inc(sy.dma_start(sb_gateT[:], gateT_in), 16)
                    return
                for tau in range(2):
                    SXR[tau % 2].inc(sy.dma_start(sb_xr[tau % 2][:], x[tau * 128:(tau + 1) * 128, :]), 16)
                for c in range(NCH):
                    for half in range(2):
                        for tt in range(2):
                            tau = 4 * c + 2 * half + tt + 2
                            if tau < NTAU:
                                sy.wait_ge(S_SQ.h, tau - 1)
                                sy.wait_ge(V_Y1.h, tau - 1)
                                SXR[tau % 2].inc(sy.dma_start(sb_xr[tau % 2][:], x[tau * 128:(tau + 1) * 128, :]), 16)
                        for tt in range(2):
                            tau = 4 * c + 2 * half + tt
                            sy.wait_ge(V_Y1.h, tau + 1)
                            SY1NW[tau % 2].inc(sy.dma_start(y1nat[tau * 128:(tau + 1) * 128, :], sb_y1[tau % 2][:]), 16)
                    sy.wait_ge(V_POOL.h, fidx(c, fj_y1t(3, 7)) + 1)
                    SY1TW.inc(sy.dma_start(y1Ts[c], sb_y1T[:]), 16)

            @blk.gpsimd
            def _(gp):
                if not test_ffn_only:
                    SSETG.inc(gp.dma_start(sb_wqkvT[:], wqkvT32), 16)
                    SSETG.inc(gp.dma_start(sb_woutT[:], woutT32), 16)
                SSETG.inc(gp.dma_start(sb_wr[:], wrT32), 16)
                SSETG.inc(gp.dma_start(sb_bprojT[:], bprojT32), 16)
                if test_ffn_only:
                    for dst, src in cast_jobs:
                        SCAST.inc(gp.dma_start(dst, src), 16)
                    return
                gp.wait_ge(SIDX.h, 16)
                cj = 0
                for tau in range(NTAU):
                    if tau >= 3:
                        tp = tau - 3
                        gp.wait_ge(PE_F.h, fidx(tp // 4, 8 * (tp % 4 + 1)))
                    SG[tau % 3].inc(gp.indirect_dma_start(
                        sb_xg[tau % 3][:], None, x,
                        IndirectOffsetOnAxis(sb_idxT[:, tau:tau + 1], 0)), 16)
                    ncast = 2 if tau < 12 else 1
                    for _ in range(ncast):
                        if cj < len(cast_jobs):
                            dst, src = cast_jobs[cj]
                            SCAST.inc(gp.dma_start(dst, src), 16)
                            cj += 1
                while cj < len(cast_jobs):
                    dst, src = cast_jobs[cj]
                    SCAST.inc(gp.dma_start(dst, src), 16)
                    cj += 1

            if not test_ffn_only:
                @blk.tensor
                def _(pe):
                    wait_setup(pe)
                    for c in range(NCH):
                        for tt in range(TPC):
                            tau = 4 * c + tt
                            pe.wait_ge(SG[tau % 3].h, 16 * (tau // 3 + 1))
                            for k in range(KT):
                                n = fidx(c, fj_xgt(tt, k))
                                if n >= 2:
                                    pe.wait_ge(V_POOL.h, n - 1)
                                PE_F.inc(pe.transpose(
                                    pool_ps[n % 2][:, 0:128],
                                    sb_xg[tau % 3][:, k * 128:(k + 1) * 128],
                                    sb_ident32[:]))
                        pe.wait_ge(V_POOL.h, fidx(c, 32))
                        for ft in range(16):
                            n = fidx(c, 32 + ft)
                            pe.wait_ge(V_POOL.h, n - 1)
                            for k in range(KT):
                                inst = pe.matmul(
                                    pool_ps[n % 2][:, :],
                                    sb_wqkvT[:, k * 3072 + ft * 128: k * 3072 + (ft + 1) * 128],
                                    sb_xgT[:, k * CH:(k + 1) * CH],
                                    start=(k == 0), stop=(k == KT - 1))
                            PE_F.inc(inst)
                        for seg in range(4):
                            for nh in range(2):
                                n = fidx(c, fj_vt(seg, nh))
                                pe.wait_ge(V_POOL.h, n - 1)
                                for k in range(KT):
                                    inst = pe.matmul(
                                        pool_ps[n % 2][0:64, :],
                                        sb_xgT[:, k * CH + seg * 128: k * CH + (seg + 1) * 128: 2],
                                        sb_wqkvT[:, k * 3072 + 2048 + nh * 512: k * 3072 + 2048 + (nh + 1) * 512],
                                        start=(k == 0), stop=(k == KT - 1))
                                PE_F.inc(inst)
                        pe.wait_ge(V_POOL.h, fidx(c, 56))
                        for sl in range(4):
                            for h in range(H):
                                i = 64 * c + 16 * sl + h
                                pb = (h % 2) * 64
                                qb = (h // 2) * CH + sl * 128
                                if i >= 2:
                                    pe.wait_ge(S_EXP.h, i - 1)
                                PE_SC.inc(pe.matmul(
                                    sc_ps[i % 2][:, 0:64],
                                    sb_qT[pb:pb + 64, qb:qb + 128],
                                    sb_kT[pb:pb + 64, qb:qb + 128:2],
                                    start=True, stop=True))
                                pe.wait_ge(V_ABF.h, i + 1)
                                if i >= 2:
                                    pe.wait_ge(V_ATS.h, i - 1)
                                PE_AT.inc(pe.transpose(
                                    at_ps[i % 2][0:64, 0:128], sb_abf[i % 2][:], sb_identbf[:]))
                                pe.wait_ge(V_ATS.h, i + 1)
                                if i >= 2:
                                    pe.wait_ge(V_OTS.h, i - 1)
                                PE_OT.inc(pe.matmul(
                                    ot_ps[i % 2][0:64, 0:128],
                                    sb_vnat[0:64, sl * 1024 + h * 64: sl * 1024 + (h + 1) * 64],
                                    sb_atT[i % 2][0:64, :],
                                    start=True, stop=True))
                        pe.wait_ge(V_OTS.h, 64 * (c + 1))
                        for tt in range(TPC):
                            tau = 4 * c + tt
                            for dh in range(2):
                                n = fidx(c, fj_wo(tt, dh))
                                pe.wait_ge(V_POOL.h, n - 1)
                                for k in range(KT):
                                    inst = pe.matmul(
                                        pool_ps[n % 2][:, :],
                                        sb_oT[:, k * CH + tt * 128: k * CH + (tt + 1) * 128],
                                        sb_woutT[:, k * 1024 + dh * 512: k * 1024 + (dh + 1) * 512],
                                        start=(k == 0), stop=(k == KT - 1))
                                PE_F.inc(inst)
                            pe.wait_ge(V_Y1.h, tau + 1)
                            for k in range(KT):
                                n = fidx(c, fj_y1t(tt, k))
                                pe.wait_ge(V_POOL.h, n - 1)
                                PE_F.inc(pe.transpose(
                                    pool_ps[n % 2][:, 0:128],
                                    sb_y1[tau % 2][:, k * 128:(k + 1) * 128],
                                    sb_ident32[:]))
                            n = fidx(c, fj_rt(tt))
                            pe.wait_ge(V_POOL.h, n)
                            for k in range(KT):
                                inst = pe.matmul(
                                    pool_ps[n % 2][:, 0:8],
                                    sb_y1T[:, k * CH + tt * 128: k * CH + (tt + 1) * 128],
                                    sb_wr[:, k * 8:(k + 1) * 8],
                                    start=(k == 0), stop=(k == KT - 1))
                            PE_F.inc(inst)
                            n = fidx(c, fj_gt(tt))
                            pe.wait_ge(V_POOL.h, n - 1)
                            pe.wait_ge(V_GATE.h, tau + 1)
                            PE_F.inc(pe.transpose(
                                pool_ps[n % 2][0:9, 0:128],
                                sb_gaug[:, tt * 9:(tt + 1) * 9],
                                sb_ident32[:]))

                @blk.vector
                def _(v):
                    wait_setup(v)
                    for tt in range(TPC):
                        v.memset(sb_gaug[:, tt * 9: tt * 9 + 1], 1.0)
                    for c in range(NCH):
                        for tt in range(TPC):
                            for k in range(KT):
                                n = fidx(c, fj_xgt(tt, k))
                                v.wait_ge(PE_F.h, n + 1)
                                V_POOL.inc(v.tensor_copy(
                                    sb_xgT[:, k * CH + tt * 128: k * CH + (tt + 1) * 128],
                                    pool_ps[n % 2][:, 0:128]))
                        for ft in range(8):
                            n = fidx(c, fj_qt(ft))
                            v.wait_ge(PE_F.h, n + 1)
                            V_POOL.inc(v.tensor_scalar(
                                sb_qT[:, ft * CH:(ft + 1) * CH], pool_ps[n % 2][:, :],
                                QSCALE, None, ALU.mult))
                        for ft in range(8):
                            n = fidx(c, fj_kt(ft))
                            v.wait_ge(PE_F.h, n + 1)
                            V_POOL.inc(v.tensor_copy(
                                sb_kT[:, ft * CH:(ft + 1) * CH], pool_ps[n % 2][:, :]))
                        for seg in range(4):
                            for nh in range(2):
                                n = fidx(c, fj_vt(seg, nh))
                                v.wait_ge(PE_F.h, n + 1)
                                V_POOL.inc(v.tensor_copy(
                                    sb_vnat[0:64, seg * 1024 + nh * 512: seg * 1024 + (nh + 1) * 512],
                                    pool_ps[n % 2][0:64, :]))
                        for sl in range(4):
                            for h in range(H):
                                i = 64 * c + 16 * sl + h
                                v.wait_ge(PE_SC.h, i + 1)
                                if i >= 2:
                                    v.wait_ge(V_ABF.h, i - 1)
                                V_NM.inc(v.tensor_reduce(
                                    sb_ast[i % 2][:, 0:1], sc_ps[i % 2][:, 0:64],
                                    AX.X, ALU.max, negate=True))
                                v.wait_ge(S_EXP.h, i + 1)
                                nvs = VS.inc(v.reciprocal(sb_ast[i % 2][:, 2:3], sb_ast[i % 2][:, 1:2]))
                                v.wait_ge(VS.h, nvs)
                                V_ABF.inc(v.tensor_scalar(
                                    sb_abf[i % 2][:], sb_pfp[i % 2][:],
                                    sb_ast[i % 2][:, 2:3], None, ALU.mult))
                                v.wait_ge(PE_AT.h, i + 1)
                                V_ATS.inc(v.tensor_copy(sb_atT[i % 2][0:64, :], at_ps[i % 2][0:64, 0:128]))
                                v.wait_ge(PE_OT.h, i + 1)
                                pb = (h % 2) * 64
                                V_OTS.inc(v.tensor_copy(
                                    sb_oT[pb:pb + 64, (h // 2) * CH + sl * 128: (h // 2) * CH + (sl + 1) * 128],
                                    ot_ps[i % 2][0:64, 0:128]))
                        for tt in range(TPC):
                            tau = 4 * c + tt
                            lp = tau % 2
                            rt = sb_xr[lp]
                            v.wait_ge(SXR[tau % 2].h, 16 * (tau // 2 + 1))
                            if tau >= 2:
                                v.wait_ge(S_SQ.h, tau - 1)
                                v.wait_ge(S_RSTD.h, tau - 1)
                            for dh in range(2):
                                n = fidx(c, fj_wo(tt, dh))
                                v.wait_ge(PE_F.h, n + 1)
                                V_POOL.inc(v.tensor_tensor(
                                    rt[:, dh * 512:(dh + 1) * 512],
                                    pool_ps[n % 2][:, :],
                                    rt[:, dh * 512:(dh + 1) * 512], ALU.add))
                            v.wait_ge(V_POOL.h, V_POOL.n)
                            nvs = VS.inc(v.tensor_reduce(sb_lst[lp][:, 0:1], rt[:], AX.X, ALU.add))
                            v.wait_ge(VS.h, nvs)
                            nvs = VS.inc(v.tensor_scalar(
                                sb_lst[lp][:, 1:2], sb_lst[lp][:, 0:1], -1.0 / D, None, ALU.mult))
                            v.wait_ge(VS.h, nvs)
                            V_D.inc(v.tensor_scalar(
                                rt[:], rt[:], sb_lst[lp][:, 1:2], None, ALU.add))
                            v.wait_ge(S_RSTD.h, tau + 1)
                            if tau >= 2:
                                pe_prev = fidx((tau - 2) // 4, fj_y1t((tau - 2) % 4, 7)) + 1
                                v.wait_ge(PE_F.h, pe_prev)
                                v.wait_ge(SY1NW[tau % 2].h, 16 * (tau // 2))
                            V_Y1.inc(v.tensor_scalar(
                                sb_y1[tau % 2][:], rt[:], sb_lst[lp][:, 4:5], None, ALU.mult))
                            for k in range(KT):
                                n = fidx(c, fj_y1t(tt, k))
                                v.wait_ge(PE_F.h, n + 1)
                                V_POOL.inc(v.tensor_copy(
                                    sb_y1T[:, k * CH + tt * 128: k * CH + (tt + 1) * 128],
                                    pool_ps[n % 2][:, 0:128]))
                            n = fidx(c, fj_rt(tt))
                            v.wait_ge(PE_F.h, n + 1)
                            if tau >= 2:
                                v.wait_ge(S_EXPR.h, tau - 1)
                            V_POOL.inc(v.tensor_copy(sb_lg[lp][:], pool_ps[n % 2][:, 0:8]))
                            v.wait_ge(V_POOL.h, V_POOL.n)
                            V_NMR.inc(v.tensor_reduce(
                                sb_rst[lp][:, 0:1], sb_lg[lp][:], AX.X, ALU.max, negate=True))
                            v.wait_ge(S_EXPR.h, tau + 1)
                            et = sb_rexp[lp]
                            rtmp = sb_rtmp[lp]
                            nvs = VS.inc(v.tensor_reduce(sb_rst[lp][:, 1:2], et[:], AX.X, ALU.max))
                            v.wait_ge(VS.h, nvs)
                            nvs = VS.inc(v.tensor_scalar(
                                rtmp[:], et[:], sb_rst[lp][:, 1:2], None, ALU.is_lt))
                            v.wait_ge(VS.h, nvs)
                            nvs = VS.inc(v.tensor_tensor(rtmp[:], rtmp[:], et[:], ALU.mult))
                            v.wait_ge(VS.h, nvs)
                            nvs = VS.inc(v.tensor_reduce(sb_rst[lp][:, 2:3], rtmp[:], AX.X, ALU.max))
                            v.wait_ge(VS.h, nvs)
                            nvs = VS.inc(v.tensor_scalar(
                                rtmp[:], et[:], sb_rst[lp][:, 2:3], None, ALU.is_ge))
                            v.wait_ge(VS.h, nvs)
                            nvs = VS.inc(v.tensor_tensor(rtmp[:], rtmp[:], et[:], ALU.mult))
                            v.wait_ge(VS.h, nvs)
                            nvs = VS.inc(v.tensor_reduce(sb_rst[lp][:, 3:4], rtmp[:], AX.X, ALU.add))
                            v.wait_ge(VS.h, nvs)
                            nvs = VS.inc(v.reciprocal(sb_rst[lp][:, 4:5], sb_rst[lp][:, 3:4]))
                            v.wait_ge(VS.h, nvs)
                            nvs = VS.inc(v.tensor_scalar(
                                sb_gaten[:, tau * 8:(tau + 1) * 8], rtmp[:],
                                sb_rst[lp][:, 4:5], None, ALU.mult))
                            v.wait_ge(VS.h, nvs)
                            V_GATE.inc(v.tensor_copy(
                                sb_gaug[:, tt * 9 + 1:(tt + 1) * 9],
                                sb_gaten[:, tau * 8:(tau + 1) * 8]))
                            n = fidx(c, fj_gt(tt))
                            v.wait_ge(PE_F.h, n + 1)
                            if c + 1 < NCH:
                                pass
                            V_POOL.inc(v.tensor_copy(
                                sb_gateT[0:9, c * CH + tt * 128: c * CH + (tt + 1) * 128],
                                pool_ps[n % 2][0:9, 0:128]))
                        if c + 1 < NCH:
                            v.wait_ge(SY1TW.h, 16 * (c + 1))

                @blk.scalar
                def _(s):
                    wait_setup(s)
                    for c in range(NCH):
                        for sl in range(4):
                            for h in range(H):
                                i = 64 * c + 16 * sl + h
                                s.wait_ge(V_NM.h, i + 1)
                                if i >= 2:
                                    s.wait_ge(V_ABF.h, i - 1)
                                S_EXP.inc(s.activation(
                                    sb_pfp[i % 2][:], sc_ps[i % 2][:, 0:64], AF.Exp,
                                    bias=sb_ast[i % 2][:, 0:1], scale=1.0,
                                    accum_out=sb_ast[i % 2][:, 1:2]))
                        for tt in range(TPC):
                            tau = 4 * c + tt
                            lp = tau % 2
                            s.wait_ge(V_D.h, tau + 1)
                            S_SQ.inc(s.activation(
                                sb_junk[:], sb_xr[lp][:], AF.Square,
                                accum_out=sb_lst[lp][:, 2:3]))
                            s.wait_ge(S_SQ.h, S_SQ.n)
                            nss = SS.inc(s.activation(
                                sb_lst[lp][:, 3:4], sb_lst[lp][:, 2:3], AF.Ln,
                                bias=sb_cst[:, 0:1], scale=sb_cst[:, 2:3]))
                            s.wait_ge(SS.h, nss)
                            S_RSTD.inc(s.activation(
                                sb_lst[lp][:, 4:5], sb_lst[lp][:, 3:4], AF.Exp,
                                scale=sb_cst[:, 1:2]))
                            s.wait_ge(V_NMR.h, tau + 1)
                            if tau >= 2:
                                s.wait_ge(V_GATE.h, tau - 1)
                            S_EXPR.inc(s.activation(
                                sb_rexp[lp][:], sb_lg[lp][:], AF.Exp,
                                bias=sb_rst[lp][:, 0:1], scale=1.0))

    nc.all_engine_barrier()

    with contextlib.ExitStack() as stB:
        sb_y1g = stB.enter_context(nc.sbuf_tensor("sb_y1g", [128, KT * CH], BF16))
        sb_h = stB.enter_context(nc.sbuf_tensor("sb_h", [128, MT * CH], BF16))
        sb_acc = stB.enter_context(nc.sbuf_tensor("sb_acc", [128, TPC * D], F32))
        ring1 = [stB.enter_context(nc.sbuf_tensor(f"ring1_{i}", [128, KT * 128], BF16)) for i in range(4)]
        ring2 = [stB.enter_context(nc.sbuf_tensor(f"ring2_{i}", [128, 4 * 1024], BF16)) for i in range(3)]
        sb_y1n = stB.enter_context(nc.sbuf_tensor("sb_y1n", [128, TPC * D], F32))
        sb_rf = [stB.enter_context(nc.sbuf_tensor(f"sb_rf{i}", [128, D], F32)) for i in range(2)]
        sb_outt = [stB.enter_context(nc.sbuf_tensor(f"sb_outt{i}", [128, D], F32)) for i in range(2)]
        sb_junk2 = stB.enter_context(nc.sbuf_tensor("sb_junk2", [128, D], F32))
        sb_lst2 = [stB.enter_context(nc.sbuf_tensor(f"sb_lst2_{i}", [128, 8], F32)) for i in range(2)]
        ff_ps = [stB.enter_context(nc.psum_tensor(f"ff_ps{i}", [128, 512], F32)) for i in range(8)]

        with nc.Block() as blk:
            @blk.sync
            def _(sy):
                for g in range(NCH):
                    if g >= 1:
                        sy.wait_ge(PE_S1.h, 32 * 9 * g)
                    if not test_ffn_only:
                        sy.wait_ge(SY1TW.h, 16 * (g + 1))
                    SY1G.inc(sy.dma_start(sb_y1g[:], y1Ts[g]), 16)
                    if g >= 1:
                        for tt in range(TPC):
                            taup = 4 * (g - 1) + tt
                            sy.wait_ge(V_OUTR.h, taup + 1)
                            SOUT[taup % 2].inc(sy.dma_start(
                                out[taup * 128:(taup + 1) * 128, :], sb_outt[taup % 2][:]), 16)
                        sy.wait_ge(V_FIN.h, 4 * g)
                    if not test_ffn_only:
                        sy.wait_ge(SY1NW[0].h, 16 * (2 * g + 2))
                        sy.wait_ge(SY1NW[1].h, 16 * (2 * g + 2))
                    SYN.inc(sy.dma_start(
                        sb_y1n[:].rearrange("p (tt d) -> p tt d", tt=TPC),
                        y1nat[g * CH:(g + 1) * CH, :].rearrange("(tt p) d -> p tt d", p=128)), 16)
                    for e in range(NE):
                        P = 9 * g + e
                        if g == 0 and e == 0:
                            sy.wait_ge(SCAST.h, 16 * 4 * NE)
                        for m in range(MT):
                            w1seq = 32 * P + m
                            if w1seq >= 4:
                                sy.wait_ge(PE_S1.h, w1seq - 3)
                            SW1[w1seq % 4].inc(sy.dma_start(ring1[w1seq % 4][:], w1_slice(e, m)), 16)
                        for q in range(8):
                            w2seq = 8 * P + q
                            if w2seq >= 3:
                                sy.wait_ge(W2FREE.h, w2seq - 2)
                            SW2[w2seq % 3].inc(sy.dma_start(ring2[w2seq % 3][:], w2_slice(e, q)), 16)
                g = NCH
                for tt in range(TPC):
                    taup = 4 * (g - 1) + tt
                    sy.wait_ge(V_OUTR.h, taup + 1)
                    SOUT[taup % 2].inc(sy.dma_start(
                        out[taup * 128:(taup + 1) * 128, :], sb_outt[taup % 2][:]), 16)
                sy.wait_ge(SOUT[0].h, 16 * (NTAU // 2))
                sy.wait_ge(SOUT[1].h, 16 * (NTAU // 2))
                for s in [SIDX, SY1TW, SY1NW[0], SY1NW[1], SXR[0], SXR[1],
                          SG[0], SG[1], SG[2], SCAST]:
                    if s.n > 0:
                        sy.wait_ge(s.h, s.n)

            @blk.tensor
            def _(pe):
                wait_setup(pe)
                for g in range(NCH):
                    for e in range(NE):
                        P = 9 * g + e
                        for m in range(MT):
                            w1seq = 32 * P + m
                            pe.wait_ge(SW1[w1seq % 4].h, 16 * (w1seq // 4 + 1))
                            if m == 0:
                                pe.wait_ge(SY1G.h, 16 * (g + 1))
                            if m >= 8:
                                pe.wait_ge(S_GELU.h, 32 * P + m - 7)
                            elif P >= 1:
                                pe.wait_ge(V_FFE.h, 8 * (P - 1) + m + 1)
                            bank = (40 * P + m) % 8
                            for k in range(KT):
                                inst = pe.matmul(
                                    ff_ps[bank][:, :],
                                    ring1[w1seq % 4][:, k * 128:(k + 1) * 128],
                                    sb_y1g[:, k * CH:(k + 1) * CH],
                                    start=(k == 0), stop=(k == KT - 1))
                                if m == 0 and k == 0 and P >= 1:
                                    W2FREE.inc(inst)
                            PE_S1.inc(inst)
                        pe.wait_ge(S_GELU.h, 32 * (P + 1))
                        if e == 0:
                            for j in range(8):
                                tt, dh = j // 2, j % 2
                                pe.matmul(
                                    ff_ps[j][:, :],
                                    sb_gateT[0:9, g * CH + tt * 128: g * CH + (tt + 1) * 128],
                                    sb_bprojT[0:9, dh * 512:(dh + 1) * 512],
                                    start=True, stop=False)
                        for k in range(32):
                            q, kk = k // 4, k % 4
                            if kk == 0:
                                w2seq = 8 * P + q
                                pe.wait_ge(SW2[w2seq % 3].h, 16 * (w2seq // 3 + 1))
                            for j in range(8):
                                tt, dh = j // 2, j % 2
                                inst = pe.matmul(
                                    ff_ps[j][:, :],
                                    sb_h[:, k * CH + tt * 128: k * CH + (tt + 1) * 128],
                                    ring2[(8 * P + q) % 3][:, kk * 1024 + dh * 512: kk * 1024 + (dh + 1) * 512],
                                    start=(e != 0 and k == 0), stop=(k == 31))
                                if k == 31:
                                    PE_S2.inc(inst)
                                elif kk == 0 and j == 0 and k > 0:
                                    W2FREE.inc(inst)

            @blk.scalar
            def _(s):
                wait_setup(s)
                for g in range(NCH):
                    for e in range(NE):
                        P = 9 * g + e
                        for m in range(MT):
                            s.wait_ge(PE_S1.h, 32 * P + m + 1)
                            bank = (40 * P + m) % 8
                            S_GELU.inc(s.activation(
                                sb_h[:, m * CH:(m + 1) * CH], ff_ps[bank][:, :],
                                AF.Gelu_apprx_tanh,
                                bias=sb_bfcT[:, e * MT + m: e * MT + m + 1], scale=1.0))
                    for tt in range(TPC):
                        tau = 4 * g + tt
                        lp = tau % 2
                        s.wait_ge(V_D2.h, tau + 1)
                        S_SQ2.inc(s.activation(
                            sb_junk2[:], sb_rf[lp][:], AF.Square,
                            accum_out=sb_lst2[lp][:, 2:3]))
                        s.wait_ge(S_SQ2.h, S_SQ2.n)
                        nss = SS.inc(s.activation(
                            sb_lst2[lp][:, 3:4], sb_lst2[lp][:, 2:3], AF.Ln,
                            bias=sb_cst[:, 0:1], scale=sb_cst[:, 2:3]))
                        s.wait_ge(SS.h, nss)
                        S_RSTD2.inc(s.activation(
                            sb_lst2[lp][:, 4:5], sb_lst2[lp][:, 3:4], AF.Exp,
                            scale=sb_cst[:, 1:2]))

            @blk.vector
            def _(v):
                wait_setup(v)
                acc_last = {}
                for g in range(NCH):
                    for e in range(NE):
                        P = 9 * g + e
                        for j in range(8):
                            tt, dh = j // 2, j % 2
                            v.wait_ge(PE_S2.h, 8 * P + j + 1)
                            accsl = sb_acc[:, tt * D + dh * 512: tt * D + (dh + 1) * 512]
                            if e == 0:
                                V_FFE.inc(v.tensor_copy(accsl, ff_ps[j][:, :]))
                            else:
                                v.wait_ge(V_FFE.h, acc_last[j])
                                tau = 4 * g + tt
                                V_FFE.inc(v.scalar_tensor_tensor(
                                    accsl, ff_ps[j][:, :],
                                    sb_gaten[:, tau * 8 + (e - 1): tau * 8 + e],
                                    accsl, ALU.mult, ALU.add))
                            acc_last[j] = V_FFE.n
                    v.wait_ge(SYN.h, 16 * (g + 1))
                    for tt in range(TPC):
                        tau = 4 * g + tt
                        lp = tau % 2
                        rf = sb_rf[lp]
                        v.wait_ge(V_FFE.h, max(acc_last[2 * tt], acc_last[2 * tt + 1]))
                        if tau >= 2:
                            v.wait_ge(S_SQ2.h, tau - 1)
                            v.wait_ge(S_RSTD2.h, tau - 1)
                            v.wait_ge(SOUT[tau % 2].h, 16 * (tau // 2))
                        nvs = VS.inc(v.tensor_tensor(
                            rf[:, 0:512], sb_acc[:, tt * D: tt * D + 512],
                            sb_y1n[:, tt * D: tt * D + 512], ALU.add))
                        V_FIN.inc(v.tensor_tensor(
                            rf[:, 512:1024], sb_acc[:, tt * D + 512: (tt + 1) * D],
                            sb_y1n[:, tt * D + 512: (tt + 1) * D], ALU.add))
                        v.wait_ge(VS.h, nvs)
                        v.wait_ge(V_FIN.h, V_FIN.n)
                        nvs = VS.inc(v.tensor_reduce(sb_lst2[lp][:, 0:1], rf[:], AX.X, ALU.add))
                        v.wait_ge(VS.h, nvs)
                        nvs = VS.inc(v.tensor_scalar(
                            sb_lst2[lp][:, 1:2], sb_lst2[lp][:, 0:1], -1.0 / D, None, ALU.mult))
                        v.wait_ge(VS.h, nvs)
                        V_D2.inc(v.tensor_scalar(
                            rf[:], rf[:], sb_lst2[lp][:, 1:2], None, ALU.add))
                        v.wait_ge(S_RSTD2.h, tau + 1)
                        V_OUTR.inc(v.tensor_scalar(
                            sb_outt[tau % 2][:], rf[:], sb_lst2[lp][:, 4:5], None, ALU.mult))

    return nc


def kernel(L0_attn_x, L0_attn_w_qkv, L0_attn_w_out, L0_ffn_Wfc_s, L0_ffn_bfc_s,
           L0_ffn_Wproj_s, L0_ffn_bproj_s, L0_ffn_Wr, L0_ffn_Wfc, L0_ffn_bfc,
           L0_ffn_Wproj, L0_ffn_bproj):
    xin = np.ascontiguousarray(np.asarray(L0_attn_x, dtype=np.float32))
    w_qkv = np.asarray(L0_attn_w_qkv, dtype=np.float32)
    w_out = np.asarray(L0_attn_w_out, dtype=np.float32)
    Wfc_s = np.asarray(L0_ffn_Wfc_s, dtype=np.float32)
    bfc_s = np.asarray(L0_ffn_bfc_s, dtype=np.float32)
    Wproj_s = np.asarray(L0_ffn_Wproj_s, dtype=np.float32)
    bproj_s = np.asarray(L0_ffn_bproj_s, dtype=np.float32)
    Wr = np.asarray(L0_ffn_Wr, dtype=np.float32)
    Wfc = np.asarray(L0_ffn_Wfc, dtype=np.float32)
    bfc = np.asarray(L0_ffn_bfc, dtype=np.float32)
    Wproj = np.asarray(L0_ffn_Wproj, dtype=np.float32)
    bproj = np.asarray(L0_ffn_bproj, dtype=np.float32)

    perm = hilbert_perm(T)
    idxT = np.ascontiguousarray(perm.reshape(NTAU, 128).T)
    ident32 = np.eye(128, dtype=np.float32)
    identbf = np.eye(128, dtype=BF)
    wqkvT32 = np.ascontiguousarray(
        w_qkv.T.reshape(KT, 128, 3072).transpose(1, 0, 2).reshape(128, KT * 3072))
    woutT32 = np.ascontiguousarray(
        w_out.T.reshape(KT, 128, 1024).transpose(1, 0, 2).reshape(128, KT * 1024))
    wrT32 = np.ascontiguousarray(
        Wr.reshape(KT, 128, 8).transpose(1, 0, 2).reshape(128, KT * 8))
    bfc_all = np.concatenate([bfc_s[None], bfc], axis=0)
    bfcT32 = np.ascontiguousarray(
        bfc_all.reshape(NE, MT, 128).transpose(2, 0, 1).reshape(128, NE * MT))
    bprojT32 = np.ascontiguousarray(np.concatenate([bproj_s[None], bproj], axis=0))
    W1_all = np.concatenate([Wfc_s[None], Wfc], axis=0)
    w1t = np.ascontiguousarray(
        W1_all.reshape(NE, KT, 128, MT, 128).transpose(0, 3, 2, 1, 4)).reshape(NE, 2, D * HID // 2)
    W2_all = np.concatenate([Wproj_s[None], Wproj], axis=0)
    w2t = np.ascontiguousarray(
        W2_all.reshape(NE, 8, 4, 128, 1024).transpose(0, 1, 3, 2, 4)).reshape(NE, 2, D * HID // 2)

    nc = build_program()
    cst = np.zeros((128, 8), dtype=np.float32)
    cst[:, 0] = EPS
    cst[:, 1] = -0.5
    cst[:, 2] = 1.0 / D
    shared = {
        "idxT": idxT, "ident32": ident32, "identbf": identbf, "cst32": cst,
        "wqkvT32": wqkvT32, "woutT32": woutT32, "wrT32": wrT32,
        "bfcT32": bfcT32, "bprojT32": bprojT32, "w1t": w1t, "w2t": w2t,
    }
    in_maps = []
    for b in range(B):
        m = dict(shared)
        m["x"] = np.ascontiguousarray(xin[b])
        in_maps.append(m)
    res = run_bass_kernel_spmd(nc, in_maps, core_ids=list(range(B)))
    return np.stack([r["out"] for r in res.results], axis=0)


# version 02
# speedup vs baseline: 1.0303x; 1.0303x over previous
import math
import numpy as np
import ml_dtypes
import concourse.bass as bass
import concourse.mybir as mybir
from concourse.bass_utils import run_bass_kernel_spmd
from concourse.bass_core_rust import IndirectOffsetOnAxis

F32 = mybir.dt.float32
BF16 = mybir.dt.bfloat16
I32 = mybir.dt.int32
AF = mybir.ActivationFunctionType
ALU = mybir.AluOpType
AX = mybir.AxisListType
BF = ml_dtypes.bfloat16

B = 8
T = 8192
D = 1024
KT = 8
HID = 4096
MT = 32
NE = 9
H = 16
HD = 64
CH = 512
NCH = 16
TPC = 4
EPS = 1e-5
QSCALE = float(HD) ** -0.5
NTAU = NCH * TPC
POOL_PC = 104


def hilbert_perm(M: int) -> np.ndarray:
    g = int(math.ceil(math.sqrt(M)))
    grid = np.arange(g * g).reshape(g, g)
    grid[1::2] = grid[1::2, ::-1]
    flat = grid.reshape(-1)
    flat = flat[flat < M][:M]
    return flat.astype(np.int32)


def fj_xgt(tt, k): return 8 * tt + k
def fj_qt(ft): return 32 + ft
def fj_kt(ft): return 40 + ft
def fj_vt(seg, nh): return 48 + 2 * seg + nh
def fj_wo(tt, dh): return 56 + 12 * tt + dh
def fj_y1t(tt, k): return 56 + 12 * tt + 2 + k
def fj_rt(tt): return 56 + 12 * tt + 10
def fj_gt(tt): return 56 + 12 * tt + 11


class CSem:
    def __init__(self, nc, name):
        self.h = nc.alloc_semaphore(name)
        self.n = 0

    def inc(self, inst, k=1):
        inst.then_inc(self.h, k)
        self.n += k
        return self.n


def build_program(test_ffn_only=False):
    nc = bass.Bass("TRN2", target_bir_lowering=False)

    x = nc.dram_tensor("x", [T, D], F32, kind="ExternalInput").ap()
    idxT = nc.dram_tensor("idxT", [128, NTAU], I32, kind="ExternalInput").ap()
    ident32 = nc.dram_tensor("ident32", [128, 128], F32, kind="ExternalInput").ap()
    identbf = nc.dram_tensor("identbf", [128, 128], BF16, kind="ExternalInput").ap()
    wqkvT32 = nc.dram_tensor("wqkvT32", [128, KT * 3072], F32, kind="ExternalInput").ap()
    woutT32 = nc.dram_tensor("woutT32", [128, KT * 1024], F32, kind="ExternalInput").ap()
    wrT32 = nc.dram_tensor("wrT32", [128, KT * 8], F32, kind="ExternalInput").ap()
    bfcT32 = nc.dram_tensor("bfcT32", [128, NE * MT], F32, kind="ExternalInput").ap()
    cst32 = nc.dram_tensor("cst32", [128, 8], F32, kind="ExternalInput").ap()
    bprojT32 = nc.dram_tensor("bprojT32", [9, D], F32, kind="ExternalInput").ap()
    WHALF = D * HID // 2
    w1t = nc.dram_tensor("w1t", [NE, 2, WHALF], F32, kind="ExternalInput").ap()
    w2t = nc.dram_tensor("w2t", [NE, 2, WHALF], F32, kind="ExternalInput").ap()
    out = nc.dram_tensor("out", [T, D], F32, kind="ExternalOutput").ap()

    w1s = nc.dram_tensor("w1s", [NE, 2, WHALF], BF16, kind="Internal").ap()
    w2s = nc.dram_tensor("w2s", [NE, 2, WHALF], BF16, kind="Internal").ap()

    def w1_chunk4(e, mc):
        h, ml = divmod(mc, 4)
        blk = 4 * KT * 128 * 128
        return w1s[e, h][ml * blk:(ml + 1) * blk].rearrange(
            "(m p kc) -> p m kc", m=4, p=128)

    def w2_slice(e, q):
        h, ql = divmod(q, 4)
        blk = 4 * 1024 * 128
        return w2s[e, h][ql * blk:(ql + 1) * blk].rearrange("(p c) -> p c", p=128)
    y1kind = "ExternalInput" if test_ffn_only else "Internal"
    y1Ts = nc.dram_tensor("y1Ts", [NCH, 128, KT * CH], BF16, kind=y1kind).ap()
    y1nat = nc.dram_tensor("y1nat", [T, D], F32, kind=y1kind).ap()
    if test_ffn_only:
        gate_in = nc.dram_tensor("gate_in", [128, NTAU * 8], F32, kind="ExternalInput").ap()
        gateT_in = nc.dram_tensor("gateT_in", [9, T], BF16, kind="ExternalInput").ap()

    SSETS = CSem(nc, "SSETS")
    SSETG = CSem(nc, "SSETG")
    SIDX = CSem(nc, "SIDX")
    SG = [CSem(nc, f"SG{i}") for i in range(3)]
    SXR = [CSem(nc, f"SXR{i}") for i in range(2)]
    SCAST = CSem(nc, "SCAST")
    SY1TW = CSem(nc, "SY1TW")
    SY1NW = [CSem(nc, f"SY1NW{i}") for i in range(2)]
    SW1 = [CSem(nc, f"SW1_{i}") for i in range(3)]
    SW2 = [CSem(nc, f"SW2_{i}") for i in range(3)]
    SY1G = CSem(nc, "SY1G")
    SYN = CSem(nc, "SYN")
    SOUT = [CSem(nc, f"SOUT{i}") for i in range(2)]

    PE_F = CSem(nc, "PE_F")
    PE_SC = CSem(nc, "PE_SC")
    PE_AT = CSem(nc, "PE_AT")
    PE_OT = CSem(nc, "PE_OT")
    V_POOL = CSem(nc, "V_POOL")
    V_NM = CSem(nc, "V_NM")
    S_EXP = CSem(nc, "S_EXP")
    V_ABF = CSem(nc, "V_ABF")
    V_ATS = CSem(nc, "V_ATS")
    V_OTS = CSem(nc, "V_OTS")
    V_D = CSem(nc, "V_D")
    S_SQ = CSem(nc, "S_SQ")
    S_RSTD = CSem(nc, "S_RSTD")
    V_Y1 = CSem(nc, "V_Y1")
    V_NMR = CSem(nc, "V_NMR")
    S_EXPR = CSem(nc, "S_EXPR")
    V_GATE = CSem(nc, "V_GATE")
    VS = CSem(nc, "VS")
    SS = CSem(nc, "SS")

    PE_S1 = CSem(nc, "PE_S1")
    S_GELU = CSem(nc, "S_GELU")
    PE_S2 = CSem(nc, "PE_S2")
    V_FFE = CSem(nc, "V_FFE")
    W2FREE = CSem(nc, "W2FREE")
    V_D2 = CSem(nc, "V_D2")
    S_SQ2 = CSem(nc, "S_SQ2")
    S_RSTD2 = CSem(nc, "S_RSTD2")
    V_OUTR = CSem(nc, "V_OUTR")
    V_FIN = CSem(nc, "V_FIN")

    SETS_TGT = 16 * (6 if test_ffn_only else 4)
    SETG_TGT = 16 * (2 if test_ffn_only else 4)

    def wait_setup(eng):
        eng.wait_ge(SSETS.h, SETS_TGT)
        eng.wait_ge(SSETG.h, SETG_TGT)

    sb_ident32 = nc.alloc_sbuf_tensor("sb_ident32", [128, 128], F32)
    sb_identbf = nc.alloc_sbuf_tensor("sb_identbf", [128, 128], BF16)
    sb_idxT = nc.alloc_sbuf_tensor("sb_idxT", [128, NTAU], I32)
    sb_bfcT = nc.alloc_sbuf_tensor("sb_bfcT", [128, NE * MT], F32)
    sb_cst = nc.alloc_sbuf_tensor("sb_cst", [128, 8], F32)
    sb_bprojT = nc.alloc_sbuf_tensor("sb_bprojT", [9, D], BF16)
    sb_wr = nc.alloc_sbuf_tensor("sb_wr", [128, KT * 8], BF16)
    sb_gaten = nc.alloc_sbuf_tensor("sb_gaten", [128, NTAU * 8], F32)
    sb_gateT = nc.alloc_sbuf_tensor("sb_gateT", [9, T], BF16)

    def _flat2d(ap):
        return ap.rearrange("(r c) -> r c", c=65536)

    cast_jobs = []
    for e in range(NE):
        cast_jobs.append((_flat2d(w1s[e, 0]), _flat2d(w1t[e, 0])))
        cast_jobs.append((_flat2d(w1s[e, 1]), _flat2d(w1t[e, 1])))
        cast_jobs.append((_flat2d(w2s[e, 0]), _flat2d(w2t[e, 0])))
        cast_jobs.append((_flat2d(w2s[e, 1]), _flat2d(w2t[e, 1])))

    def fidx(c, j):
        return POOL_PC * c + j

    import contextlib

    with contextlib.ExitStack() as stA:
        if not test_ffn_only:
            sb_wqkvT = stA.enter_context(nc.sbuf_tensor("sb_wqkvT", [128, KT * 3072], BF16))
            sb_woutT = stA.enter_context(nc.sbuf_tensor("sb_woutT", [128, KT * 1024], BF16))
            sb_xg = [stA.enter_context(nc.sbuf_tensor(f"sb_xg{i}", [128, D], F32)) for i in range(3)]
            sb_xgT = stA.enter_context(nc.sbuf_tensor("sb_xgT", [128, KT * CH], BF16))
            sb_qT = stA.enter_context(nc.sbuf_tensor("sb_qT", [128, KT * CH], BF16))
            sb_kT = stA.enter_context(nc.sbuf_tensor("sb_kT", [128, KT * CH], BF16))
            sb_vnat = stA.enter_context(nc.sbuf_tensor("sb_vnat", [64, 4096], BF16))
            sb_pfp = [stA.enter_context(nc.sbuf_tensor(f"sb_pfp{i}", [128, 64], F32)) for i in range(2)]
            sb_abf = [stA.enter_context(nc.sbuf_tensor(f"sb_abf{i}", [128, 64], BF16)) for i in range(2)]
            sb_atT = [stA.enter_context(nc.sbuf_tensor(f"sb_atT{i}", [64, 128], BF16)) for i in range(2)]
            sb_oT = stA.enter_context(nc.sbuf_tensor("sb_oT", [128, KT * CH], BF16))
            sb_xr = [stA.enter_context(nc.sbuf_tensor(f"sb_xr{i}", [128, D], F32)) for i in range(2)]
            sb_junk = stA.enter_context(nc.sbuf_tensor("sb_junk", [128, D], F32))
            sb_y1 = [stA.enter_context(nc.sbuf_tensor(f"sb_y1{i}", [128, D], F32)) for i in range(2)]
            sb_y1T = stA.enter_context(nc.sbuf_tensor("sb_y1T", [128, KT * CH], BF16))
            sb_ast = [stA.enter_context(nc.sbuf_tensor(f"sb_ast{i}", [128, 4], F32)) for i in range(2)]
            sb_lst = [stA.enter_context(nc.sbuf_tensor(f"sb_lst{i}", [128, 8], F32)) for i in range(2)]
            sb_lg = [stA.enter_context(nc.sbuf_tensor(f"sb_lg{i}", [128, 8], F32)) for i in range(2)]
            sb_rexp = [stA.enter_context(nc.sbuf_tensor(f"sb_rexp{i}", [128, 8], F32)) for i in range(2)]
            sb_rtmp = [stA.enter_context(nc.sbuf_tensor(f"sb_rtmp{i}", [128, 8], F32)) for i in range(2)]
            sb_rst = [stA.enter_context(nc.sbuf_tensor(f"sb_rst{i}", [128, 8], F32)) for i in range(2)]
            sb_gaug = stA.enter_context(nc.sbuf_tensor("sb_gaug", [128, 4 * 9], F32))
            pool_ps = [stA.enter_context(nc.psum_tensor(f"pool_ps{i}", [128, 512], F32)) for i in range(2)]
            sc_ps = [stA.enter_context(nc.psum_tensor(f"sc_ps{i}", [128, 512], F32)) for i in range(2)]
            at_ps = [stA.enter_context(nc.psum_tensor(f"at_ps{i}", [128, 1024], BF16)) for i in range(2)]
            ot_ps = [stA.enter_context(nc.psum_tensor(f"ot_ps{i}", [128, 512], F32)) for i in range(2)]

        with nc.Block() as blk:
            @blk.sync
            def _(sy):
                SSETS.inc(sy.dma_start(sb_ident32[:], ident32), 16)
                SSETS.inc(sy.dma_start(sb_identbf[:], identbf), 16)
                SIDX.inc(sy.dma_start(sb_idxT[:], idxT), 16)
                SSETS.inc(sy.dma_start(sb_bfcT[:], bfcT32), 16)
                SSETS.inc(sy.dma_start(sb_cst[:], cst32), 16)
                if test_ffn_only:
                    SSETS.inc(sy.dma_start(sb_gaten[:], gate_in), 16)
                    SSETS.inc(sy.dma_start(sb_gateT[:], gateT_in), 16)
                    return
                for tau in range(2):
                    SXR[tau % 2].inc(sy.dma_start(sb_xr[tau % 2][:], x[tau * 128:(tau + 1) * 128, :]), 16)
                for c in range(NCH):
                    for half in range(2):
                        for tt in range(2):
                            tau = 4 * c + 2 * half + tt + 2
                            if tau < NTAU:
                                sy.wait_ge(S_SQ.h, tau - 1)
                                sy.wait_ge(V_Y1.h, tau - 1)
                                SXR[tau % 2].inc(sy.dma_start(sb_xr[tau % 2][:], x[tau * 128:(tau + 1) * 128, :]), 16)
                        for tt in range(2):
                            tau = 4 * c + 2 * half + tt
                            sy.wait_ge(V_Y1.h, tau + 1)
                            SY1NW[tau % 2].inc(sy.dma_start(y1nat[tau * 128:(tau + 1) * 128, :], sb_y1[tau % 2][:]), 16)
                    sy.wait_ge(V_POOL.h, fidx(c, fj_y1t(3, 7)) + 1)
                    SY1TW.inc(sy.dma_start(y1Ts[c], sb_y1T[:]), 16)
                if True:
                    for s in [SXR[0], SXR[1], SY1NW[0], SY1NW[1], SY1TW]:
                        sy.wait_ge(s.h, s.n)

            @blk.gpsimd
            def _(gp):
                if not test_ffn_only:
                    SSETG.inc(gp.dma_start(sb_wqkvT[:], wqkvT32), 16)
                    SSETG.inc(gp.dma_start(sb_woutT[:], woutT32), 16)
                SSETG.inc(gp.dma_start(sb_wr[:], wrT32), 16)
                SSETG.inc(gp.dma_start(sb_bprojT[:], bprojT32), 16)
                if test_ffn_only:
                    for dst, src in cast_jobs:
                        SCAST.inc(gp.dma_start(dst, src), 16)
                    return
                gp.wait_ge(SIDX.h, 16)
                cj = 0
                for tau in range(NTAU):
                    if tau >= 3:
                        tp = tau - 3
                        gp.wait_ge(PE_F.h, fidx(tp // 4, 8 * (tp % 4 + 1)))
                    SG[tau % 3].inc(gp.indirect_dma_start(
                        sb_xg[tau % 3][:], None, x,
                        IndirectOffsetOnAxis(sb_idxT[:, tau:tau + 1], 0)), 16)
                    ncast = 2 if tau < 12 else 1
                    for _ in range(ncast):
                        if cj < len(cast_jobs):
                            dst, src = cast_jobs[cj]
                            SCAST.inc(gp.dma_start(dst, src), 16)
                            cj += 1
                while cj < len(cast_jobs):
                    dst, src = cast_jobs[cj]
                    SCAST.inc(gp.dma_start(dst, src), 16)
                    cj += 1
                for s in SG:
                    gp.wait_ge(s.h, s.n)

            if not test_ffn_only:
                @blk.tensor
                def _(pe):
                    wait_setup(pe)
                    for c in range(NCH):
                        for tt in range(TPC):
                            tau = 4 * c + tt
                            pe.wait_ge(SG[tau % 3].h, 16 * (tau // 3 + 1))
                            for k in range(KT):
                                n = fidx(c, fj_xgt(tt, k))
                                if n >= 2:
                                    pe.wait_ge(V_POOL.h, n - 1)
                                PE_F.inc(pe.transpose(
                                    pool_ps[n % 2][:, 0:128],
                                    sb_xg[tau % 3][:, k * 128:(k + 1) * 128],
                                    sb_ident32[:]))
                        pe.wait_ge(V_POOL.h, fidx(c, 32))
                        for ft in range(16):
                            n = fidx(c, 32 + ft)
                            pe.wait_ge(V_POOL.h, n - 1)
                            for k in range(KT):
                                inst = pe.matmul(
                                    pool_ps[n % 2][:, :],
                                    sb_wqkvT[:, k * 3072 + ft * 128: k * 3072 + (ft + 1) * 128],
                                    sb_xgT[:, k * CH:(k + 1) * CH],
                                    start=(k == 0), stop=(k == KT - 1))
                            PE_F.inc(inst)
                        for seg in range(4):
                            for nh in range(2):
                                n = fidx(c, fj_vt(seg, nh))
                                pe.wait_ge(V_POOL.h, n - 1)
                                for k in range(KT):
                                    inst = pe.matmul(
                                        pool_ps[n % 2][0:64, :],
                                        sb_xgT[:, k * CH + seg * 128: k * CH + (seg + 1) * 128: 2],
                                        sb_wqkvT[:, k * 3072 + 2048 + nh * 512: k * 3072 + 2048 + (nh + 1) * 512],
                                        start=(k == 0), stop=(k == KT - 1))
                                PE_F.inc(inst)
                        pe.wait_ge(V_POOL.h, fidx(c, 56))
                        for sl in range(4):
                            for h in range(H):
                                i = 64 * c + 16 * sl + h
                                pb = (h % 2) * 64
                                qb = (h // 2) * CH + sl * 128
                                if i >= 2:
                                    pe.wait_ge(S_EXP.h, i - 1)
                                PE_SC.inc(pe.matmul(
                                    sc_ps[i % 2][:, 0:64],
                                    sb_qT[pb:pb + 64, qb:qb + 128],
                                    sb_kT[pb:pb + 64, qb:qb + 128:2],
                                    start=True, stop=True))
                                pe.wait_ge(V_ABF.h, i + 1)
                                if i >= 2:
                                    pe.wait_ge(V_ATS.h, i - 1)
                                PE_AT.inc(pe.transpose(
                                    at_ps[i % 2][0:64, 0:128], sb_abf[i % 2][:], sb_identbf[:]))
                                pe.wait_ge(V_ATS.h, i + 1)
                                if i >= 2:
                                    pe.wait_ge(V_OTS.h, i - 1)
                                PE_OT.inc(pe.matmul(
                                    ot_ps[i % 2][0:64, 0:128],
                                    sb_vnat[0:64, sl * 1024 + h * 64: sl * 1024 + (h + 1) * 64],
                                    sb_atT[i % 2][0:64, :],
                                    start=True, stop=True))
                        pe.wait_ge(V_OTS.h, 64 * (c + 1))
                        for tt in range(TPC):
                            tau = 4 * c + tt
                            for dh in range(2):
                                n = fidx(c, fj_wo(tt, dh))
                                pe.wait_ge(V_POOL.h, n - 1)
                                for k in range(KT):
                                    inst = pe.matmul(
                                        pool_ps[n % 2][:, :],
                                        sb_oT[:, k * CH + tt * 128: k * CH + (tt + 1) * 128],
                                        sb_woutT[:, k * 1024 + dh * 512: k * 1024 + (dh + 1) * 512],
                                        start=(k == 0), stop=(k == KT - 1))
                                PE_F.inc(inst)
                            pe.wait_ge(V_Y1.h, tau + 1)
                            for k in range(KT):
                                n = fidx(c, fj_y1t(tt, k))
                                pe.wait_ge(V_POOL.h, n - 1)
                                PE_F.inc(pe.transpose(
                                    pool_ps[n % 2][:, 0:128],
                                    sb_y1[tau % 2][:, k * 128:(k + 1) * 128],
                                    sb_ident32[:]))
                            n = fidx(c, fj_rt(tt))
                            pe.wait_ge(V_POOL.h, n)
                            for k in range(KT):
                                inst = pe.matmul(
                                    pool_ps[n % 2][:, 0:8],
                                    sb_y1T[:, k * CH + tt * 128: k * CH + (tt + 1) * 128],
                                    sb_wr[:, k * 8:(k + 1) * 8],
                                    start=(k == 0), stop=(k == KT - 1))
                            PE_F.inc(inst)
                            n = fidx(c, fj_gt(tt))
                            pe.wait_ge(V_POOL.h, n - 1)
                            pe.wait_ge(V_GATE.h, tau + 1)
                            PE_F.inc(pe.transpose(
                                pool_ps[n % 2][0:9, 0:128],
                                sb_gaug[:, tt * 9:(tt + 1) * 9],
                                sb_ident32[:]))

                @blk.vector
                def _(v):
                    wait_setup(v)
                    for tt in range(TPC):
                        v.memset(sb_gaug[:, tt * 9: tt * 9 + 1], 1.0)
                    for c in range(NCH):
                        for tt in range(TPC):
                            for k in range(KT):
                                n = fidx(c, fj_xgt(tt, k))
                                v.wait_ge(PE_F.h, n + 1)
                                V_POOL.inc(v.tensor_copy(
                                    sb_xgT[:, k * CH + tt * 128: k * CH + (tt + 1) * 128],
                                    pool_ps[n % 2][:, 0:128]))
                        for ft in range(8):
                            n = fidx(c, fj_qt(ft))
                            v.wait_ge(PE_F.h, n + 1)
                            V_POOL.inc(v.tensor_scalar(
                                sb_qT[:, ft * CH:(ft + 1) * CH], pool_ps[n % 2][:, :],
                                QSCALE, None, ALU.mult))
                        for ft in range(8):
                            n = fidx(c, fj_kt(ft))
                            v.wait_ge(PE_F.h, n + 1)
                            V_POOL.inc(v.tensor_copy(
                                sb_kT[:, ft * CH:(ft + 1) * CH], pool_ps[n % 2][:, :]))
                        for seg in range(4):
                            for nh in range(2):
                                n = fidx(c, fj_vt(seg, nh))
                                v.wait_ge(PE_F.h, n + 1)
                                V_POOL.inc(v.tensor_copy(
                                    sb_vnat[0:64, seg * 1024 + nh * 512: seg * 1024 + (nh + 1) * 512],
                                    pool_ps[n % 2][0:64, :]))
                        for sl in range(4):
                            for h in range(H):
                                i = 64 * c + 16 * sl + h
                                v.wait_ge(PE_SC.h, i + 1)
                                if i >= 2:
                                    v.wait_ge(V_ABF.h, i - 1)
                                V_NM.inc(v.tensor_reduce(
                                    sb_ast[i % 2][:, 0:1], sc_ps[i % 2][:, 0:64],
                                    AX.X, ALU.max, negate=True))
                                v.wait_ge(S_EXP.h, i + 1)
                                nvs = VS.inc(v.reciprocal(sb_ast[i % 2][:, 2:3], sb_ast[i % 2][:, 1:2]))
                                v.wait_ge(VS.h, nvs)
                                V_ABF.inc(v.tensor_scalar(
                                    sb_abf[i % 2][:], sb_pfp[i % 2][:],
                                    sb_ast[i % 2][:, 2:3], None, ALU.mult))
                                v.wait_ge(PE_AT.h, i + 1)
                                V_ATS.inc(v.tensor_copy(sb_atT[i % 2][0:64, :], at_ps[i % 2][0:64, 0:128]))
                                v.wait_ge(PE_OT.h, i + 1)
                                pb = (h % 2) * 64
                                V_OTS.inc(v.tensor_copy(
                                    sb_oT[pb:pb + 64, (h // 2) * CH + sl * 128: (h // 2) * CH + (sl + 1) * 128],
                                    ot_ps[i % 2][0:64, 0:128]))
                        for tt in range(TPC):
                            tau = 4 * c + tt
                            lp = tau % 2
                            rt = sb_xr[lp]
                            v.wait_ge(SXR[tau % 2].h, 16 * (tau // 2 + 1))
                            if tau >= 2:
                                v.wait_ge(S_SQ.h, tau - 1)
                                v.wait_ge(S_RSTD.h, tau - 1)
                            for dh in range(2):
                                n = fidx(c, fj_wo(tt, dh))
                                v.wait_ge(PE_F.h, n + 1)
                                V_POOL.inc(v.tensor_tensor(
                                    rt[:, dh * 512:(dh + 1) * 512],
                                    pool_ps[n % 2][:, :],
                                    rt[:, dh * 512:(dh + 1) * 512], ALU.add))
                            v.wait_ge(V_POOL.h, V_POOL.n)
                            nvs = VS.inc(v.tensor_reduce(sb_lst[lp][:, 0:1], rt[:], AX.X, ALU.add))
                            v.wait_ge(VS.h, nvs)
                            nvs = VS.inc(v.tensor_scalar(
                                sb_lst[lp][:, 1:2], sb_lst[lp][:, 0:1], -1.0 / D, None, ALU.mult))
                            v.wait_ge(VS.h, nvs)
                            V_D.inc(v.tensor_scalar(
                                rt[:], rt[:], sb_lst[lp][:, 1:2], None, ALU.add))
                            v.wait_ge(S_RSTD.h, tau + 1)
                            if tau >= 2:
                                pe_prev = fidx((tau - 2) // 4, fj_y1t((tau - 2) % 4, 7)) + 1
                                v.wait_ge(PE_F.h, pe_prev)
                                v.wait_ge(SY1NW[tau % 2].h, 16 * (tau // 2))
                            V_Y1.inc(v.tensor_scalar(
                                sb_y1[tau % 2][:], rt[:], sb_lst[lp][:, 4:5], None, ALU.mult))
                            for k in range(KT):
                                n = fidx(c, fj_y1t(tt, k))
                                v.wait_ge(PE_F.h, n + 1)
                                V_POOL.inc(v.tensor_copy(
                                    sb_y1T[:, k * CH + tt * 128: k * CH + (tt + 1) * 128],
                                    pool_ps[n % 2][:, 0:128]))
                            n = fidx(c, fj_rt(tt))
                            v.wait_ge(PE_F.h, n + 1)
                            if tau >= 2:
                                v.wait_ge(S_EXPR.h, tau - 1)
                            V_POOL.inc(v.tensor_copy(sb_lg[lp][:], pool_ps[n % 2][:, 0:8]))
                            v.wait_ge(V_POOL.h, V_POOL.n)
                            V_NMR.inc(v.tensor_reduce(
                                sb_rst[lp][:, 0:1], sb_lg[lp][:], AX.X, ALU.max, negate=True))
                            v.wait_ge(S_EXPR.h, tau + 1)
                            et = sb_rexp[lp]
                            rtmp = sb_rtmp[lp]
                            nvs = VS.inc(v.tensor_reduce(sb_rst[lp][:, 1:2], et[:], AX.X, ALU.max))
                            v.wait_ge(VS.h, nvs)
                            nvs = VS.inc(v.tensor_scalar(
                                rtmp[:], et[:], sb_rst[lp][:, 1:2], None, ALU.is_lt))
                            v.wait_ge(VS.h, nvs)
                            nvs = VS.inc(v.tensor_tensor(rtmp[:], rtmp[:], et[:], ALU.mult))
                            v.wait_ge(VS.h, nvs)
                            nvs = VS.inc(v.tensor_reduce(sb_rst[lp][:, 2:3], rtmp[:], AX.X, ALU.max))
                            v.wait_ge(VS.h, nvs)
                            nvs = VS.inc(v.tensor_scalar(
                                rtmp[:], et[:], sb_rst[lp][:, 2:3], None, ALU.is_ge))
                            v.wait_ge(VS.h, nvs)
                            nvs = VS.inc(v.tensor_tensor(rtmp[:], rtmp[:], et[:], ALU.mult))
                            v.wait_ge(VS.h, nvs)
                            nvs = VS.inc(v.tensor_reduce(sb_rst[lp][:, 3:4], rtmp[:], AX.X, ALU.add))
                            v.wait_ge(VS.h, nvs)
                            nvs = VS.inc(v.reciprocal(sb_rst[lp][:, 4:5], sb_rst[lp][:, 3:4]))
                            v.wait_ge(VS.h, nvs)
                            nvs = VS.inc(v.tensor_scalar(
                                sb_gaten[:, tau * 8:(tau + 1) * 8], rtmp[:],
                                sb_rst[lp][:, 4:5], None, ALU.mult))
                            v.wait_ge(VS.h, nvs)
                            V_GATE.inc(v.tensor_copy(
                                sb_gaug[:, tt * 9 + 1:(tt + 1) * 9],
                                sb_gaten[:, tau * 8:(tau + 1) * 8]))
                            n = fidx(c, fj_gt(tt))
                            v.wait_ge(PE_F.h, n + 1)
                            if c + 1 < NCH:
                                pass
                            V_POOL.inc(v.tensor_copy(
                                sb_gateT[0:9, c * CH + tt * 128: c * CH + (tt + 1) * 128],
                                pool_ps[n % 2][0:9, 0:128]))
                        if c + 1 < NCH:
                            v.wait_ge(SY1TW.h, 16 * (c + 1))

                @blk.scalar
                def _(s):
                    wait_setup(s)
                    for c in range(NCH):
                        for sl in range(4):
                            for h in range(H):
                                i = 64 * c + 16 * sl + h
                                s.wait_ge(V_NM.h, i + 1)
                                if i >= 2:
                                    s.wait_ge(V_ABF.h, i - 1)
                                S_EXP.inc(s.activation(
                                    sb_pfp[i % 2][:], sc_ps[i % 2][:, 0:64], AF.Exp,
                                    bias=sb_ast[i % 2][:, 0:1], scale=1.0,
                                    accum_out=sb_ast[i % 2][:, 1:2]))
                        for tt in range(TPC):
                            tau = 4 * c + tt
                            lp = tau % 2
                            s.wait_ge(V_D.h, tau + 1)
                            S_SQ.inc(s.activation(
                                sb_junk[:], sb_xr[lp][:], AF.Square,
                                accum_out=sb_lst[lp][:, 2:3]))
                            s.wait_ge(S_SQ.h, S_SQ.n)
                            nss = SS.inc(s.activation(
                                sb_lst[lp][:, 3:4], sb_lst[lp][:, 2:3], AF.Ln,
                                bias=sb_cst[:, 0:1], scale=sb_cst[:, 2:3]))
                            s.wait_ge(SS.h, nss)
                            S_RSTD.inc(s.activation(
                                sb_lst[lp][:, 4:5], sb_lst[lp][:, 3:4], AF.Exp,
                                scale=sb_cst[:, 1:2]))
                            s.wait_ge(V_NMR.h, tau + 1)
                            if tau >= 2:
                                s.wait_ge(V_GATE.h, tau - 1)
                            S_EXPR.inc(s.activation(
                                sb_rexp[lp][:], sb_lg[lp][:], AF.Exp,
                                bias=sb_rst[lp][:, 0:1], scale=1.0))

    nc.all_engine_barrier()

    with contextlib.ExitStack() as stB:
        sb_y1g = stB.enter_context(nc.sbuf_tensor("sb_y1g", [128, KT * CH], BF16))
        sb_h = stB.enter_context(nc.sbuf_tensor("sb_h", [128, MT * CH], BF16))
        sb_acc = stB.enter_context(nc.sbuf_tensor("sb_acc", [128, TPC * D], F32))
        ring1 = [stB.enter_context(nc.sbuf_tensor(f"ring1_{i}", [128, 4 * KT * 128], BF16)) for i in range(3)]
        ring2 = [stB.enter_context(nc.sbuf_tensor(f"ring2_{i}", [128, 4 * 1024], BF16)) for i in range(3)]
        sb_y1n = stB.enter_context(nc.sbuf_tensor("sb_y1n", [128, TPC * D], F32))
        sb_rf = [stB.enter_context(nc.sbuf_tensor(f"sb_rf{i}", [128, D], F32)) for i in range(2)]
        sb_outt = [stB.enter_context(nc.sbuf_tensor(f"sb_outt{i}", [128, D], F32)) for i in range(2)]
        sb_junk2 = stB.enter_context(nc.sbuf_tensor("sb_junk2", [128, D], F32))
        sb_lst2 = [stB.enter_context(nc.sbuf_tensor(f"sb_lst2_{i}", [128, 8], F32)) for i in range(2)]
        ff_ps = [stB.enter_context(nc.psum_tensor(f"ff_ps{i}", [128, 512], F32)) for i in range(8)]

        with nc.Block() as blk:
            @blk.sync
            def _(sy):
                def emit_w1_loads(g, e, mc0, mc1):
                    P = 9 * g + e
                    for mc in range(mc0, mc1):
                        w1seq = 8 * P + mc
                        if w1seq >= 3:
                            sy.wait_ge(PE_S1.h, 4 * (w1seq - 3) + 4)
                        SW1[w1seq % 3].inc(sy.dma_start(
                            ring1[w1seq % 3][:].rearrange("p (m kc) -> p m kc", m=4),
                            w1_chunk4(e, mc)), 16)

                def emit_w2_loads(g, e):
                    P = 9 * g + e
                    for q in range(8):
                        w2seq = 8 * P + q
                        if w2seq >= 3:
                            sy.wait_ge(W2FREE.h, w2seq - 2)
                        SW2[w2seq % 3].inc(sy.dma_start(ring2[w2seq % 3][:], w2_slice(e, q)), 16)

                for g in range(NCH):
                    if g >= 1:
                        sy.wait_ge(PE_S1.h, 32 * 9 * g)
                    if not test_ffn_only:
                        sy.wait_ge(SY1TW.h, 16 * (g + 1))
                    SY1G.inc(sy.dma_start(sb_y1g[:], y1Ts[g]), 16)
                    if g == 0:
                        sy.wait_ge(SCAST.h, 16 * 4 * NE)
                    emit_w1_loads(g, 0, 0, 3)
                    if g >= 1:
                        for tt in range(TPC):
                            taup = 4 * (g - 1) + tt
                            sy.wait_ge(V_OUTR.h, taup + 1)
                            SOUT[taup % 2].inc(sy.dma_start(
                                out[taup * 128:(taup + 1) * 128, :], sb_outt[taup % 2][:]), 16)
                        sy.wait_ge(V_FIN.h, 4 * g)
                    if not test_ffn_only:
                        sy.wait_ge(SY1NW[0].h, 16 * (2 * g + 2))
                        sy.wait_ge(SY1NW[1].h, 16 * (2 * g + 2))
                    SYN.inc(sy.dma_start(
                        sb_y1n[:].rearrange("p (tt d) -> p tt d", tt=TPC),
                        y1nat[g * CH:(g + 1) * CH, :].rearrange("(tt p) d -> p tt d", p=128)), 16)
                    emit_w1_loads(g, 0, 3, 8)
                    emit_w2_loads(g, 0)
                    for e in range(1, NE):
                        emit_w1_loads(g, e, 0, 8)
                        emit_w2_loads(g, e)
                g = NCH
                for tt in range(TPC):
                    taup = 4 * (g - 1) + tt
                    sy.wait_ge(V_OUTR.h, taup + 1)
                    SOUT[taup % 2].inc(sy.dma_start(
                        out[taup * 128:(taup + 1) * 128, :], sb_outt[taup % 2][:]), 16)
                sy.wait_ge(SOUT[0].h, 16 * (NTAU // 2))
                sy.wait_ge(SOUT[1].h, 16 * (NTAU // 2))
                for s in [SIDX, SY1TW, SY1NW[0], SY1NW[1], SXR[0], SXR[1],
                          SG[0], SG[1], SG[2], SCAST]:
                    if s.n > 0:
                        sy.wait_ge(s.h, s.n)

            @blk.tensor
            def _(pe):
                wait_setup(pe)
                for g in range(NCH):
                    for e in range(NE):
                        P = 9 * g + e
                        for m in range(MT):
                            if m % 4 == 0:
                                w1seq = 8 * P + m // 4
                                pe.wait_ge(SW1[w1seq % 3].h, 16 * (w1seq // 3 + 1))
                            if m == 0:
                                pe.wait_ge(SY1G.h, 16 * (g + 1))
                            if m >= 8:
                                pe.wait_ge(S_GELU.h, 32 * P + m - 7)
                            elif P >= 1:
                                pe.wait_ge(V_FFE.h, 8 * (P - 1) + m + 1)
                            bank = (40 * P + m) % 8
                            w1seq = 8 * P + m // 4
                            for k in range(KT):
                                inst = pe.matmul(
                                    ff_ps[bank][:, :],
                                    ring1[w1seq % 3][:, (m % 4) * KT * 128 + k * 128: (m % 4) * KT * 128 + (k + 1) * 128],
                                    sb_y1g[:, k * CH:(k + 1) * CH],
                                    start=(k == 0), stop=(k == KT - 1))
                                if m == 0 and k == 0 and P >= 1:
                                    W2FREE.inc(inst)
                            PE_S1.inc(inst)
                        pe.wait_ge(S_GELU.h, 32 * (P + 1))
                        if e == 0:
                            for j in range(8):
                                tt, dh = j // 2, j % 2
                                pe.matmul(
                                    ff_ps[j][:, :],
                                    sb_gateT[0:9, g * CH + tt * 128: g * CH + (tt + 1) * 128],
                                    sb_bprojT[0:9, dh * 512:(dh + 1) * 512],
                                    start=True, stop=False)
                        for k in range(32):
                            q, kk = k // 4, k % 4
                            if kk == 0:
                                w2seq = 8 * P + q
                                pe.wait_ge(SW2[w2seq % 3].h, 16 * (w2seq // 3 + 1))
                            for j in range(8):
                                tt, dh = j // 2, j % 2
                                inst = pe.matmul(
                                    ff_ps[j][:, :],
                                    sb_h[:, k * CH + tt * 128: k * CH + (tt + 1) * 128],
                                    ring2[(8 * P + q) % 3][:, kk * 1024 + dh * 512: kk * 1024 + (dh + 1) * 512],
                                    start=(e != 0 and k == 0), stop=(k == 31))
                                if k == 31:
                                    PE_S2.inc(inst)
                                elif kk == 0 and j == 0 and k > 0:
                                    W2FREE.inc(inst)

            @blk.scalar
            def _(s):
                wait_setup(s)
                for g in range(NCH):
                    for e in range(NE):
                        P = 9 * g + e
                        for m in range(MT):
                            s.wait_ge(PE_S1.h, 32 * P + m + 1)
                            bank = (40 * P + m) % 8
                            S_GELU.inc(s.activation(
                                sb_h[:, m * CH:(m + 1) * CH], ff_ps[bank][:, :],
                                AF.Gelu_apprx_tanh,
                                bias=sb_bfcT[:, e * MT + m: e * MT + m + 1], scale=1.0))
                    for tt in range(TPC):
                        tau = 4 * g + tt
                        lp = tau % 2
                        s.wait_ge(V_D2.h, tau + 1)
                        S_SQ2.inc(s.activation(
                            sb_junk2[:], sb_rf[lp][:], AF.Square,
                            accum_out=sb_lst2[lp][:, 2:3]))
                        s.wait_ge(S_SQ2.h, S_SQ2.n)
                        nss = SS.inc(s.activation(
                            sb_lst2[lp][:, 3:4], sb_lst2[lp][:, 2:3], AF.Ln,
                            bias=sb_cst[:, 0:1], scale=sb_cst[:, 2:3]))
                        s.wait_ge(SS.h, nss)
                        S_RSTD2.inc(s.activation(
                            sb_lst2[lp][:, 4:5], sb_lst2[lp][:, 3:4], AF.Exp,
                            scale=sb_cst[:, 1:2]))

            @blk.vector
            def _(v):
                wait_setup(v)
                acc_last = {}
                for g in range(NCH):
                    for e in range(NE):
                        P = 9 * g + e
                        for j in range(8):
                            tt, dh = j // 2, j % 2
                            v.wait_ge(PE_S2.h, 8 * P + j + 1)
                            accsl = sb_acc[:, tt * D + dh * 512: tt * D + (dh + 1) * 512]
                            if e == 0:
                                V_FFE.inc(v.tensor_copy(accsl, ff_ps[j][:, :]))
                            else:
                                v.wait_ge(V_FFE.h, acc_last[j])
                                tau = 4 * g + tt
                                V_FFE.inc(v.scalar_tensor_tensor(
                                    accsl, ff_ps[j][:, :],
                                    sb_gaten[:, tau * 8 + (e - 1): tau * 8 + e],
                                    accsl, ALU.mult, ALU.add))
                            acc_last[j] = V_FFE.n
                    v.wait_ge(SYN.h, 16 * (g + 1))
                    for tt in range(TPC):
                        tau = 4 * g + tt
                        lp = tau % 2
                        rf = sb_rf[lp]
                        v.wait_ge(V_FFE.h, max(acc_last[2 * tt], acc_last[2 * tt + 1]))
                        if tau >= 2:
                            v.wait_ge(S_SQ2.h, tau - 1)
                            v.wait_ge(S_RSTD2.h, tau - 1)
                            v.wait_ge(SOUT[tau % 2].h, 16 * (tau // 2))
                        nvs = VS.inc(v.tensor_tensor(
                            rf[:, 0:512], sb_acc[:, tt * D: tt * D + 512],
                            sb_y1n[:, tt * D: tt * D + 512], ALU.add))
                        V_FIN.inc(v.tensor_tensor(
                            rf[:, 512:1024], sb_acc[:, tt * D + 512: (tt + 1) * D],
                            sb_y1n[:, tt * D + 512: (tt + 1) * D], ALU.add))
                        v.wait_ge(VS.h, nvs)
                        v.wait_ge(V_FIN.h, V_FIN.n)
                        nvs = VS.inc(v.tensor_reduce(sb_lst2[lp][:, 0:1], rf[:], AX.X, ALU.add))
                        v.wait_ge(VS.h, nvs)
                        nvs = VS.inc(v.tensor_scalar(
                            sb_lst2[lp][:, 1:2], sb_lst2[lp][:, 0:1], -1.0 / D, None, ALU.mult))
                        v.wait_ge(VS.h, nvs)
                        V_D2.inc(v.tensor_scalar(
                            rf[:], rf[:], sb_lst2[lp][:, 1:2], None, ALU.add))
                        v.wait_ge(S_RSTD2.h, tau + 1)
                        V_OUTR.inc(v.tensor_scalar(
                            sb_outt[tau % 2][:], rf[:], sb_lst2[lp][:, 4:5], None, ALU.mult))

    return nc


def kernel(L0_attn_x, L0_attn_w_qkv, L0_attn_w_out, L0_ffn_Wfc_s, L0_ffn_bfc_s,
           L0_ffn_Wproj_s, L0_ffn_bproj_s, L0_ffn_Wr, L0_ffn_Wfc, L0_ffn_bfc,
           L0_ffn_Wproj, L0_ffn_bproj):
    xin = np.ascontiguousarray(np.asarray(L0_attn_x, dtype=np.float32))
    w_qkv = np.asarray(L0_attn_w_qkv, dtype=np.float32)
    w_out = np.asarray(L0_attn_w_out, dtype=np.float32)
    Wfc_s = np.asarray(L0_ffn_Wfc_s, dtype=np.float32)
    bfc_s = np.asarray(L0_ffn_bfc_s, dtype=np.float32)
    Wproj_s = np.asarray(L0_ffn_Wproj_s, dtype=np.float32)
    bproj_s = np.asarray(L0_ffn_bproj_s, dtype=np.float32)
    Wr = np.asarray(L0_ffn_Wr, dtype=np.float32)
    Wfc = np.asarray(L0_ffn_Wfc, dtype=np.float32)
    bfc = np.asarray(L0_ffn_bfc, dtype=np.float32)
    Wproj = np.asarray(L0_ffn_Wproj, dtype=np.float32)
    bproj = np.asarray(L0_ffn_bproj, dtype=np.float32)

    perm = hilbert_perm(T)
    idxT = np.ascontiguousarray(perm.reshape(NTAU, 128).T)
    ident32 = np.eye(128, dtype=np.float32)
    identbf = np.eye(128, dtype=BF)
    wqkvT32 = np.ascontiguousarray(
        w_qkv.T.reshape(KT, 128, 3072).transpose(1, 0, 2).reshape(128, KT * 3072))
    woutT32 = np.ascontiguousarray(
        w_out.T.reshape(KT, 128, 1024).transpose(1, 0, 2).reshape(128, KT * 1024))
    wrT32 = np.ascontiguousarray(
        Wr.reshape(KT, 128, 8).transpose(1, 0, 2).reshape(128, KT * 8))
    bfc_all = np.concatenate([bfc_s[None], bfc], axis=0)
    bfcT32 = np.ascontiguousarray(
        bfc_all.reshape(NE, MT, 128).transpose(2, 0, 1).reshape(128, NE * MT))
    bprojT32 = np.ascontiguousarray(np.concatenate([bproj_s[None], bproj], axis=0))
    W1_all = np.concatenate([Wfc_s[None], Wfc], axis=0)
    w1t = np.ascontiguousarray(
        W1_all.reshape(NE, KT, 128, MT, 128).transpose(0, 3, 2, 1, 4)).reshape(NE, 2, D * HID // 2)
    W2_all = np.concatenate([Wproj_s[None], Wproj], axis=0)
    w2t = np.ascontiguousarray(
        W2_all.reshape(NE, 8, 4, 128, 1024).transpose(0, 1, 3, 2, 4)).reshape(NE, 2, D * HID // 2)

    nc = build_program()
    cst = np.zeros((128, 8), dtype=np.float32)
    cst[:, 0] = EPS
    cst[:, 1] = -0.5
    cst[:, 2] = 1.0 / D
    shared = {
        "idxT": idxT, "ident32": ident32, "identbf": identbf, "cst32": cst,
        "wqkvT32": wqkvT32, "woutT32": woutT32, "wrT32": wrT32,
        "bfcT32": bfcT32, "bprojT32": bprojT32, "w1t": w1t, "w2t": w2t,
    }
    in_maps = []
    for b in range(B):
        m = dict(shared)
        m["x"] = np.ascontiguousarray(xin[b])
        in_maps.append(m)
    res = run_bass_kernel_spmd(nc, in_maps, core_ids=list(range(B)))
    return np.stack([r["out"] for r in res.results], axis=0)
